# version 02
# baseline (speedup 1.0000x reference)
import contextlib
import numpy as np
import concourse.bass as bass
import concourse.mybir as mybir
import bass_rust
from concourse import library_config
from concourse.bass_utils import run_bass_kernel_spmd

F32 = mybir.dt.float32
AF = mybir.ActivationFunctionType
AL = mybir.AluOpType
AX = mybir.AxisListType

T = 2048
D = 1024
NT = T // 128
NC = D // 128
NST = 64
SB = 32
EPS = 1e-5
BIG = 1.0e30

_STOP_STAGE = 4

ENGINES = ("tensor", "vector", "scalar", "gpsimd", "sync")


class Bld:
    def __init__(self, nc):
        self.nc = nc
        self.ops = {e: [] for e in ENGINES}
        self.cnt = {}
        self.hiw = {}
        self.sems = {}
        self.dma_sems = ("QS", "QG", "QC")
        self.dma_wait_since = {}

    def sem_names(self):
        return ["T", "V", "S", "G", "QS", "QG", "QC"]

    def produce(self, eng, fn, sem=None, n=1):
        sem = sem or {"tensor": "T", "vector": "V", "scalar": "S", "gpsimd": "G"}[eng]
        if sem in ("T", "V", "S", "G"):
            cur = self.cnt.get(sem, 0)
            if cur > self.hiw.get((eng, sem), 0):
                self.hiw[(eng, sem)] = cur
                self.ops[eng].append(("wait", sem, cur, 0))
        self.cnt[sem] = self.cnt.get(sem, 0) + n
        self.ops[eng].append(("op", fn, sem, n))
        return (sem, self.cnt[sem])

    def emit(self, eng, fn):
        self.ops[eng].append(("op", fn, None, 0))

    def dma(self, qeng, out, in_, slow=False):
        sem = {"sync": "QS", "gpsimd": "QG", "scalar": "QC"}[qeng]
        if self.dma_wait_since.get(sem):
            self.ops[qeng].append(("wait", sem, self.cnt.get(sem, 0), 0))
            self.dma_wait_since[sem] = False
        return self.produce(qeng, lambda e, out=out, in_=in_, slow=slow: e.dma_start(out=out, in_=in_, allow_slow_non_contiguous=slow), sem=sem, n=16)

    def wait(self, eng, tok):
        if tok is None:
            return
        sem, c = tok
        if sem in self.dma_sems:
            c = self.cnt.get(sem, 0)
            self.dma_wait_since[sem] = True
        if self.hiw.get((eng, sem), 0) >= c:
            return
        self.hiw[(eng, sem)] = c
        self.ops[eng].append(("wait", sem, c, 0))

    def barrier(self):
        engines = ("tensor", "vector", "scalar", "gpsimd")
        toks = {e: self.produce(e, lambda eng: eng.nop()) for e in engines}
        for e in engines:
            for o, t in toks.items():
                if o != e:
                    self.wait(e, t)
        for q in ("sync",):
            for o, t in toks.items():
                self.wait(q, t)

    def play(self, block):
        sems = self.sems
        def mk(eng_name):
            def run(eng):
                for kind, a, bb, cc in self.ops[eng_name]:
                    if kind == "op":
                        h = a(eng)
                        if bb is not None:
                            h.then_inc(sems[bb], cc)
                    else:
                        eng.wait_ge(sems[a], bb)
            return run
        for e in ENGINES:
            if self.ops[e]:
                getattr(block, e)(mk(e))


def host_constants():
    c = {}
    c["c_ident"] = np.eye(128, dtype=np.float32)
    c["c_ones"] = np.ones((128, 128), dtype=np.float32)
    c["c_kidx"] = np.tile(np.arange(SB + 1, dtype=np.float32), (NST, 1))
    krev = (SB - 1) - (np.arange(128) % SB)
    c["c_krev"] = np.tile(krev[:, None].astype(np.float32), (1, NST))
    m = np.where(np.arange(128)[None, :] >= np.arange(128)[:, None], 0.0, -BIG)
    c["c_mask"] = m.astype(np.float32)
    inv = 1.0 / (10000.0 ** (np.arange(0, 64, 2, dtype=np.float64) / 64.0))
    tpos = np.arange(T, dtype=np.float64)
    fr = np.outer(inv, tpos)
    cos = np.cos(fr); sin = np.sin(fr)
    c2 = np.zeros((128, T)); s2 = np.zeros((128, T))
    for r in range(128):
        dh = r % 64; i = dh // 2
        c2[r] = cos[i]
        s2[r] = -sin[i] if dh % 2 == 0 else sin[i]
    c["c_cos"] = c2.astype(np.float32)
    c["c_sin"] = s2.astype(np.float32)
    return c


def build(stop_stage=None):
    if stop_stage is None:
        stop_stage = _STOP_STAGE
    nc = bass.Bass("TRN2", target_bir_lowering=False)
    ap = {}
    ins_shapes = {
        "u": [T, D],
        "A_log": [NST], "Cm": [D, NST], "Dv": [D],
        "delta": [D], "gamma": [D], "beta": [D], "freq": [D], "reward": [D],
        "router": [D, 8], "wg": [8, D, 2048], "wu": [8, D, 2048], "wd": [8, 2048, D],
        "wqkv": [D, 3 * D], "wo": [D, D],
        "fw1": [D, D], "fa1": [D, D], "fa2": [D, D], "fbias": [D],
        "c_ident": [128, 128], "c_ones": [128, 128], "c_kidx": [NST, SB + 1], "c_krev": [128, NST],
        "c_mask": [128, 128], "c_cos": [128, T], "c_sin": [128, T],
    }
    for k, s in ins_shapes.items():
        ap[k] = nc.dram_tensor(k, s, F32, kind="ExternalInput").ap()
    out = nc.dram_tensor("out", [T, D], F32, kind="ExternalOutput").ap()

    b = Bld(nc)
    es = contextlib.ExitStack()
    with es:
        for _sn in b.sem_names():
            b.sems[_sn] = es.enter_context(nc.semaphore(_sn))

        def sb(name, p, f):
            return es.enter_context(nc.sbuf_tensor(name, [p, f], F32))

        A = sb("A", 128, NT * D)
        B = sb("B", 128, NC * T)
        ident = sb("ident", 128, 128)
        ones = sb("ones", 128, 128)
        smalls = sb("smalls", 128, 1704)
        stat = sb("stat", 128, 16)
        Z = sb("Z", 128, 2048)
        pb = [es.enter_context(nc.psum_tensor(f"pb{i}", [128, 512], F32)) for i in range(8)]
        zsb = Z[:, 0:1024]
        rsb = Z[:, 1024:2048]

        def bcast_row(dst_ap, src_row_ap, n, bank, wait_toks=()):
            for tk in wait_toks:
                b.wait("tensor", tk)
            b.wait("tensor", t_ones)
            o0 = 0
            ev = None
            while o0 < n:
                w = min(512, n - o0)
                tt = b.produce("tensor", lambda e, o0=o0, w=w: e.matmul(
                    bank[:, 0:w], ones[0:1, :], src_row_ap[:, o0:o0 + w],
                    start=True, stop=True))
                b.wait("vector", tt)
                ev = b.produce("vector", lambda e, o0=o0, w=w: e.tensor_copy(dst_ap[:, o0:o0 + w], bank[:, 0:w]))
                b.wait("tensor", ev)
                o0 += w
            return ev

        def ln_stats_apply(src_ap, out_ap, scale_bcast=None, bias_bcast=None):
            b.produce("vector", lambda e: e.bn_stats(stat[:, 0:6], src_ap[:, 0:512]))
            b.produce("vector", lambda e: e.bn_stats(stat[:, 6:12], src_ap[:, 512:1024]))
            b.produce("vector", lambda e: e.bn_aggr(stat[:, 12:14], stat[:, 0:12]))
            b.produce("vector", lambda e: e.tensor_scalar(stat[:, 14:15], stat[:, 13:14], EPS, None, AL.add))
            tv_rc = b.produce("vector", lambda e: e.reciprocal(stat[:, 15:16], stat[:, 14:15]))
            b.wait("scalar", tv_rc)
            ts_sq = b.produce("scalar", lambda e: e.activation(stat[:, 14:15], stat[:, 15:16], AF.Sqrt))
            b.wait("vector", ts_sq)
            tok = b.produce("vector", lambda e: e.tensor_scalar(out_ap, src_ap, stat[:, 12:13], stat[:, 14:15], AL.subtract, AL.mult))
            if scale_bcast is not None:
                tok = b.produce("vector", lambda e: e.tensor_tensor(out_ap, out_ap, scale_bcast, AL.mult))
            if bias_bcast is not None:
                tok = b.produce("vector", lambda e: e.tensor_tensor(out_ap, out_ap, bias_bcast, AL.add))
            return tok

        def transpose_tile_to_B(i, src_ap):
            for c in range(NC):
                bank = pb[c // 4]
                tt = b.produce("tensor", lambda e, c=c, bank=bank: e.transpose(
                    bank[:, (c % 4) * 128:((c % 4) + 1) * 128],
                    src_ap[:, c * 128:(c + 1) * 128], ident[:, :]))
            b.wait("vector", tt)
            for half in range(2):
                ev = b.produce("vector", lambda e, half=half, i=i: e.tensor_copy(
                    B.rearrange("p (c t) -> p c t", t=T)[:, 4 * half:4 * half + 4, i * 128:(i + 1) * 128],
                    pb[half][:, :].rearrange("p (c t) -> p c t", t=128)))
            b.wait("tensor", ev)
            return ev

        def transpose_B_tile_to(i, dst_ap):
            for c in range(NC):
                bank = pb[c // 4]
                tt = b.produce("tensor", lambda e, c=c, i=i, bank=bank: e.transpose(
                    bank[:, (c % 4) * 128:((c % 4) + 1) * 128],
                    B[:, c * T + i * 128:c * T + (i + 1) * 128], ident[:, :]))
            b.wait("vector", tt)
            for half in range(2):
                ev = b.produce("vector", lambda e, half=half: e.tensor_copy(
                    dst_ap[:, half * 512:(half + 1) * 512], pb[half][:, :]))
            b.wait("tensor", ev)
            return ev

        t_u = b.dma("sync", A.rearrange("p (n m) -> p n m", m=D), ap["u"].rearrange("(n p) m -> p n m", p=128))
        t_ident = b.dma("sync", ident[:, :], ap["c_ident"])
        t_ones = b.dma("sync", ones[:, :], ap["c_ones"])

        with contextlib.ExitStack() as ph:
            C = ph.enter_context(nc.sbuf_tensor("C_ssm", [128, 8192], F32))
            G = ph.enter_context(nc.sbuf_tensor("G_ssm", [128, 5120], F32))
            gamma_b = G[:, 0:1024]
            beta_b = G[:, 1024:2048]
            CT = G[0:NST, 2048:3072]
            Mt = G[0:NST, 3072:4096]
            Ssb = G[0:NST, 4096:5120]

            kidx = smalls[0:NST, 0:33]
            powT = smalls[0:NST, 33:66]
            krev = smalls[:, 66:130]
            rint = smalls[:, 130:194]
            lamb = smalls[:, 194:258]
            lamc = smalls[0:NST, 258:259]
            lamr = smalls[0:1, 259:323]
            kst = smalls[:, 323:579]
            k0d = smalls[:, 579:587]
            dcol = smalls[:, 587:595]
            fcol = smalls[:, 595:603]
            alogc = smalls[0:NST, 603:604]
            alogr = smalls[0:1, 604:668]
            del0 = smalls[0:1, 668:669]
            d0b = smalls[:, 669:670]
            negd0 = smalls[:, 670:671]
            dfre = smalls[:, 671:679]

            t_kidx = b.dma("sync", kidx, ap["c_kidx"])
            t_krev = b.dma("sync", krev, ap["c_krev"])
            t_alogc = b.dma("sync", alogc, ap["A_log"].rearrange("(n o) -> n o", o=1))
            t_alogr = b.dma("sync", alogr, ap["A_log"].rearrange("(o n) -> o n", o=1))
            t_del0 = b.dma("sync", del0, ap["delta"][0:1].rearrange("(o n) -> o n", o=1))
            t_dcol = b.dma("sync", dcol, ap["Dv"].rearrange("(c p) -> p c", p=128), slow=True)
            t_fcol = b.dma("sync", fcol, ap["freq"].rearrange("(c p) -> p c", p=128), slow=True)
            t_grow = b.dma("sync", G[0:1, 0:1024], ap["gamma"].rearrange("(o d) -> o d", o=1))
            t_brow = b.dma("sync", G[0:1, 1024:2048], ap["beta"].rearrange("(o d) -> o d", o=1))
            t_rew = b.dma("sync", smalls[0:1, 680:1704], ap["reward"].rearrange("(o d) -> o d", o=1))
            t_cm = b.dma("sync", B[:, 0:NC * NST].rearrange("p (c n) -> p c n", n=NST), ap["Cm"].rearrange("(c p) n -> p c n", p=128))

            tg_g = bcast_row(gamma_b, G[0:1, 0:1024], 1024, pb[3], (t_grow,))
            tg_b = bcast_row(beta_b, G[0:1, 1024:2048], 1024, pb[3], (t_brow,))

            b.wait("scalar", t_alogc)
            ts_e1 = b.produce("scalar", lambda e: e.activation(lamc, alogc, AF.Exp))
            b.wait("scalar", t_alogr)
            ts_e2 = b.produce("scalar", lambda e: e.activation(lamr, alogr, AF.Exp))
            tg_d0 = bcast_row(d0b, del0, 1, pb[3], (t_del0,))
            b.produce("vector", lambda e: e.tensor_scalar(negd0, d0b, -1.0, None, AL.mult))
            b.wait("vector", ts_e1)
            tv_lamc = b.produce("vector", lambda e: e.tensor_tensor(lamc, lamc, negd0[0:NST, :], AL.mult))
            b.wait("vector", ts_e2)
            tv_lamr = b.produce("vector", lambda e: e.tensor_scalar(lamr, lamr, negd0[0:1, :], None, AL.mult))
            tg_lamb = bcast_row(lamb, lamr, NST, pb[3], (tv_lamr,))

            b.wait("vector", t_kidx)
            tv_pk = b.produce("vector", lambda e: e.tensor_scalar(powT, kidx, lamc, None, AL.mult))
            b.wait("scalar", tv_pk)
            ts_pow = b.produce("scalar", lambda e: e.activation(powT, powT, AF.Exp))
            b.wait("vector", t_krev); b.wait("vector", tg_lamb)
            tv_rk = b.produce("vector", lambda e: e.tensor_tensor(rint, krev, lamb, AL.mult))
            b.wait("scalar", tv_rk)
            ts_rin = b.produce("scalar", lambda e: e.activation(rint, rint, AF.Exp))

            b.wait("tensor", t_cm); b.wait("tensor", t_ident)
            for c in range(NC):
                bank = pb[c // 4]
                tt = b.produce("tensor", lambda e, c=c, bank=bank: e.transpose(
                    bank[0:NST, (c % 4) * 128:((c % 4) + 1) * 128], B[:, c * NST:(c + 1) * NST], ident[:, :]))
            b.wait("vector", tt)
            tv_ct = b.produce("vector", lambda e: e.tensor_copy(CT[:, 0:512], pb[0][0:NST, :]))
            tv_ct = b.produce("vector", lambda e: e.tensor_copy(CT[:, 512:1024], pb[1][0:NST, :]))
            b.wait("tensor", tv_ct); b.wait("tensor", ts_pow)
            for c in range(NC):
                tt = b.produce("tensor", lambda e, c=c: e.matmul(
                    pb[2][:, c * 32:(c + 1) * 32], CT[0:NST, c * 128:(c + 1) * 128],
                    powT[:, 0:32], start=True, stop=True))
            b.wait("vector", tt); b.wait("vector", t_fcol)
            b.produce("vector", lambda e: e.tensor_copy(kst, pb[2][:, 0:256]))
            for c in range(NC):
                b.produce("vector", lambda e, c=c: e.tensor_scalar(
                    kst[:, 32 * c:32 * (c + 1)], kst[:, 32 * c:32 * (c + 1)], fcol[:, c:c + 1], None, AL.mult))
            kst3 = kst.rearrange("p (c k) -> p c k", k=32)
            b.wait("vector", t_dcol)
            b.produce("vector", lambda e: e.tensor_tensor(dfre, dcol, fcol, AL.mult))
            tv_k0 = b.produce("vector", lambda e: e.tensor_tensor(k0d, kst3[:, :, 0], dfre, AL.add))

            b.wait("tensor", t_u)
            for i in range(NT):
                transpose_tile_to_B(i, A[:, i * D:(i + 1) * D])

            tv_s0 = b.produce("vector", lambda e: e.memset(Ssb, 0.0))
            r32 = powT[:, SB:SB + 1]
            rexp = powT[:, 1:1 + SB]
            upd_stt = {0: None, 1: None}
            last_emit_ev = {}
            y0T3 = C.rearrange("p (c b k) -> p c b k", c=NC, k=SB)

            for half in range(2):
                hoff = half * 1024
                b.wait("vector", tv_k0)
                for c in range(NC):
                    b.produce("vector", lambda e, c=c, hoff=hoff: e.tensor_scalar(
                        C[:, c * 1024:(c + 1) * 1024], B[:, c * T + hoff:c * T + hoff + 1024],
                        k0d[:, c:c + 1], None, AL.mult))
                uT3 = B.rearrange("p (c t) -> p c t", t=T)
                for c in range(NC):
                    for dlt in range(1, SB):
                        b.produce("vector", lambda e, c=c, dlt=dlt, hoff=hoff: e.scalar_tensor_tensor(
                            y0T3[:, c, :, dlt:SB],
                            uT3[:, c, hoff:hoff + 1024].rearrange("p (bb k) -> p bb k", k=SB)[:, :, 0:SB - dlt],
                            kst[:, 32 * c + dlt:32 * c + dlt + 1],
                            y0T3[:, c, :, dlt:SB], AL.mult, AL.add))

                for it in range(8):
                    i = half * 8 + it
                    for q in range(4):
                        g = i * 4 + q
                        b.wait("vector", tv_s0)
                        tv_mt = b.produce("vector", lambda e: e.tensor_tensor(Mt, CT, Ssb, AL.mult))
                        tail_bank = pb[4 + (g % 2)]
                        tok = last_emit_ev.get(g % 2)
                        if tok is not None:
                            b.wait("tensor", tok)
                        b.wait("tensor", tv_mt)
                        for c in range(NC):
                            tt = b.produce("tensor", lambda e, c=c, tail_bank=tail_bank: e.matmul(
                                tail_bank[:, c * 32:(c + 1) * 32], Mt[0:NST, c * 128:(c + 1) * 128],
                                rexp, start=True, stop=True))
                        b.wait("vector", tt)
                        for c in range(NC):
                            evt = b.produce("vector", lambda e, c=c, it=it, q=q, tail_bank=tail_bank: e.scalar_tensor_tensor(
                                C[:, c * 1024 + it * 128 + q * SB: c * 1024 + it * 128 + (q + 1) * SB],
                                tail_bank[:, c * 32:(c + 1) * 32], fcol[:, c:c + 1],
                                C[:, c * 1024 + it * 128 + q * SB: c * 1024 + it * 128 + (q + 1) * SB],
                                AL.mult, AL.add))
                        last_emit_ev[g % 2] = evt
                        ub = (pb[6], pb[7]) if (g % 2) == 0 else (pb[2], pb[3])
                        if upd_stt[g % 2] is not None:
                            b.wait("tensor", upd_stt[g % 2])
                        b.wait("tensor", ts_rin)
                        for h in range(2):
                            tt2 = b.produce("tensor", lambda e, i=i, q=q, h=h, ub=ub: e.matmul(
                                ub[h][0:NST, 0:512], rint[q * SB:(q + 1) * SB, :],
                                A[q * SB:(q + 1) * SB, i * D + h * 512:i * D + (h + 1) * 512],
                                start=True, stop=True, tile_position=(q * SB, 0)))
                        b.wait("vector", tt2)
                        for h in range(2):
                            _stt = b.produce("vector", lambda e, h=h, ub=ub: e.scalar_tensor_tensor(
                                Ssb[:, h * 512:(h + 1) * 512], Ssb[:, h * 512:(h + 1) * 512], r32,
                                ub[h][0:NST, 0:512], AL.mult, AL.add))
                        upd_stt[g % 2] = _stt

                for it in range(8):
                    i = half * 8 + it
                    for c in range(NC):
                        bank = pb[c // 4]
                        tt = b.produce("tensor", lambda e, c=c, it=it, bank=bank: e.transpose(
                            bank[:, (c % 4) * 128:((c % 4) + 1) * 128],
                            C[:, c * 1024 + it * 128:c * 1024 + (it + 1) * 128], ident[:, :]))
                    b.wait("vector", tt)
                    for hh in range(2):
                        tv_z = b.produce("vector", lambda e, hh=hh: e.tensor_copy(zsb[:, hh * 512:(hh + 1) * 512], pb[hh][:, :]))
                    b.wait("tensor", tv_z)
                    if i == 0:
                        b.wait("vector", t_rew)
                        b.produce("vector", lambda e: e.tensor_tensor(zsb[0:1, :], zsb[0:1, :], smalls[0:1, 680:1704], AL.add))
                    b.wait("vector", tg_g); b.wait("vector", tg_b)
                    ln_stats_apply(zsb, zsb, gamma_b, beta_b)
                    b.produce("vector", lambda e, i=i: e.tensor_tensor(rsb, zsb, A[:, i * D:(i + 1) * D], AL.add))
                    tok = ln_stats_apply(rsb, A[:, i * D:(i + 1) * D])
                    b.wait("tensor", tok)
                    transpose_tile_to_B(i, A[:, i * D:(i + 1) * D])
            b.barrier()

        if stop_stage >= 2:
            with contextlib.ExitStack() as ph:
                C2 = ph.enter_context(nc.sbuf_tensor("C_moe", [128, 12288], F32))
                W = ph.enter_context(nc.sbuf_tensor("W_moe", [128, 4096], F32))
                accq = C2[:, 0:4096]
                hq = C2[:, 4096:12288]
                fw_all = smalls[:, 64:192]
                scr = smalls[:, 192:256]

                t_rt = b.dma("sync", smalls[:, 0:64].rearrange("p (c e) -> p c e", e=8), ap["router"].rearrange("(c p) e -> p c e", p=128))
                b.wait("tensor", t_rt)
                tvcp = None
                for i in range(NT):
                    if tvcp is not None:
                        b.wait("tensor", tvcp)
                    for c in range(NC):
                        tt = b.produce("tensor", lambda e, i=i, c=c: e.matmul(
                            pb[6][:, 0:8], B[:, c * T + i * 128:c * T + (i + 1) * 128],
                            smalls[:, c * 8:(c + 1) * 8], start=(c == 0), stop=(c == NC - 1)))
                    b.wait("vector", tt)
                    tvcp = b.produce("vector", lambda e: e.tensor_copy(scr[:, 0:8], pb[6][:, 0:8]))
                    tvv = b.produce("vector", lambda e: e.tensor_reduce(scr[:, 8:9], scr[:, 0:8], AX.X, AL.max, negate=True))
                    b.wait("scalar", tvv)
                    tss = b.produce("scalar", lambda e: e.activation(scr[:, 0:8], scr[:, 0:8], AF.Exp, bias=scr[:, 8:9]))
                    b.wait("vector", tss)
                    b.produce("vector", lambda e: e.tensor_reduce(scr[:, 9:10], scr[:, 0:8], AX.X, AL.add))
                    b.produce("vector", lambda e: e.reciprocal(scr[:, 10:11], scr[:, 9:10]))
                    b.produce("vector", lambda e: e.tensor_scalar(scr[:, 0:8], scr[:, 0:8], scr[:, 10:11], None, AL.mult))
                    b.produce("vector", lambda e: e.tensor_reduce(scr[:, 8:9], scr[:, 0:8], AX.X, AL.max))
                    b.produce("vector", lambda e: e.tensor_scalar(scr[:, 16:24], scr[:, 0:8], scr[:, 8:9], None, AL.is_equal))
                    b.produce("vector", lambda e: e.scalar_tensor_tensor(scr[:, 24:32], scr[:, 16:24], -BIG, scr[:, 0:8], AL.mult, AL.add))
                    b.produce("vector", lambda e: e.tensor_reduce(scr[:, 9:10], scr[:, 24:32], AX.X, AL.max))
                    b.produce("vector", lambda e: e.tensor_scalar(scr[:, 24:32], scr[:, 24:32], scr[:, 9:10], None, AL.is_equal))
                    tvv = b.produce("vector", lambda e: e.tensor_tensor(scr[:, 11:12], scr[:, 9:10], scr[:, 8:9], AL.subtract))
                    b.wait("scalar", tvv)
                    tss = b.produce("scalar", lambda e: e.activation(scr[:, 11:12], scr[:, 11:12], AF.Exp))
                    b.wait("vector", tss)
                    b.produce("vector", lambda e: e.tensor_scalar(scr[:, 12:13], scr[:, 11:12], 1.0, None, AL.add))
                    b.produce("vector", lambda e: e.reciprocal(scr[:, 13:14], scr[:, 12:13]))
                    b.produce("vector", lambda e: e.tensor_tensor(scr[:, 14:15], scr[:, 11:12], scr[:, 13:14], AL.mult))
                    b.produce("vector", lambda e, i=i: e.tensor_scalar(fw_all[:, i * 8:(i + 1) * 8], scr[:, 16:24], scr[:, 13:14], None, AL.mult))
                    tvf = b.produce("vector", lambda e, i=i: e.scalar_tensor_tensor(
                        fw_all[:, i * 8:(i + 1) * 8], scr[:, 24:32], scr[:, 14:15], fw_all[:, i * 8:(i + 1) * 8], AL.mult, AL.add))
                b.wait("tensor", tvf)

                wg8 = W[:, 0:2048]
                wu8 = W[:, 2048:4096]
                last_w_use = None
                dn_ev = None
                tv_h = None
                for tq in range(4):
                    tv_acc0 = b.produce("vector", lambda e: e.memset(accq, 0.0))
                    for ex in range(8):
                        for fe in range(8):
                            b.wait("sync", last_w_use)
                            tok_wg = b.dma("sync", wg8.rearrange("p (c f) -> p c f", f=256), ap["wg"][ex, :, fe * 256:(fe + 1) * 256].rearrange("(c p) f -> p c f", p=128))
                            tok_wu = b.dma("sync", wu8.rearrange("p (c f) -> p c f", f=256), ap["wu"][ex, :, fe * 256:(fe + 1) * 256].rearrange("(c p) f -> p c f", p=128))
                            b.wait("tensor", tok_wu)
                            if dn_ev is not None:
                                b.wait("tensor", dn_ev)
                            for fs in range(2):
                                F0 = fe * 2 + fs
                                if tv_h is not None:
                                    b.wait("tensor", tv_h)
                                for c in range(NC):
                                    ttg = b.produce("tensor", lambda e, c=c, fs=fs, tq=tq: e.matmul(
                                        pb[0][:, 0:512], wg8[:, c * 256 + fs * 128:c * 256 + (fs + 1) * 128],
                                        B[:, c * T + tq * 512:c * T + (tq + 1) * 512],
                                        start=(c == 0), stop=(c == NC - 1)))
                                for c in range(NC):
                                    ttu = b.produce("tensor", lambda e, c=c, fs=fs, tq=tq: e.matmul(
                                        pb[1][:, 0:512], wu8[:, c * 256 + fs * 128:c * 256 + (fs + 1) * 128],
                                        B[:, c * T + tq * 512:c * T + (tq + 1) * 512],
                                        start=(c == 0), stop=(c == NC - 1)))
                                last_w_use = ttu
                                b.wait("scalar", ttg)
                                ts_si = b.produce("scalar", lambda e: e.activation(rsb[:, 0:512], pb[0][:, 0:512], AF.Silu))
                                b.wait("vector", ttu); b.wait("vector", ts_si)
                                tv_h = b.produce("vector", lambda e, F0=F0: e.tensor_tensor(
                                    hq[:, F0 * 512:(F0 + 1) * 512], pb[1][:, 0:512], rsb[:, 0:512], AL.mult))
                        b.wait("tensor", tv_h)
                        for fed in range(8):
                            b.wait("sync", last_w_use)
                            tok_wd = b.dma("sync", wg8.rearrange("p (s d) -> p s d", d=1024), ap["wd"][ex, fed * 256:(fed + 1) * 256, :].rearrange("(s p) d -> p s d", p=128))
                            b.wait("tensor", tok_wd)
                            if fed == 0 and dn_ev is not None:
                                b.wait("tensor", dn_ev)
                            for s in range(2):
                                F0 = fed * 2 + s
                                for ti in range(4):
                                    for dh in range(2):
                                        ttd = b.produce("tensor", lambda e, F0=F0, ti=ti, dh=dh, s=s: e.matmul(
                                            pb[ti * 2 + dh][:, 0:512],
                                            hq[:, F0 * 512 + ti * 128:F0 * 512 + (ti + 1) * 128],
                                            wg8[:, s * 1024 + dh * 512:s * 1024 + (dh + 1) * 512],
                                            start=(F0 == 0), stop=(F0 == 15)))
                            last_w_use = ttd
                        b.wait("vector", ttd)
                        for ti in range(4):
                            for dh in range(2):
                                dn_ev = b.produce("vector", lambda e, ti=ti, dh=dh, tq=tq, ex=ex: e.scalar_tensor_tensor(
                                    accq[:, ti * 1024 + dh * 512:ti * 1024 + (dh + 1) * 512],
                                    pb[ti * 2 + dh][:, 0:512],
                                    fw_all[:, (tq * 4 + ti) * 8 + ex:(tq * 4 + ti) * 8 + ex + 1],
                                    accq[:, ti * 1024 + dh * 512:ti * 1024 + (dh + 1) * 512],
                                    AL.mult, AL.add))
                    b.wait("tensor", dn_ev)
                    for ti in range(4):
                        i = tq * 4 + ti
                        ev = transpose_B_tile_to(i, zsb)
                        b.produce("vector", lambda e, ti=ti: e.tensor_tensor(rsb, zsb, accq[:, ti * 1024:(ti + 1) * 1024], AL.add))
                        tok = ln_stats_apply(rsb, A[:, i * D:(i + 1) * D])
                        b.wait("tensor", tok)
                        transpose_tile_to_B(i, A[:, i * D:(i + 1) * D])
                b.barrier()

        if stop_stage >= 3:
            with contextlib.ExitStack() as ph:
                C3 = ph.enter_context(nc.sbuf_tensor("C_att", [128, 8224], F32))
                W3 = ph.enter_context(nc.sbuf_tensor("W_att", [128, 4096], F32))
                G3 = ph.enter_context(nc.sbuf_tensor("G_att", [128, 4096], F32))
                QT = C3[:, 0:2048]
                KT = C3[:, 2048:4096]
                VA = C3[:, 4096:6176]
                tmp3 = C3[:, 6176:8224]
                cosb = G3[:, 0:2048]
                sinb = G3[:, 2048:4096]
                maskb = smalls[:, 256:384]
                negb = smalls[:, 384:385]
                bsc = smalls[:, 385:388]
                o_sb = smalls[:, 400:464]
                oT_sb = smalls[0:64, 464:592]
                PT = Z[:, 0:2048]
                SWAP = [1, 0, 3, 2, 5, 4, 7, 6, 9, 8, 11, 10, 13, 12, 15, 14,
                        17, 16, 19, 18, 21, 20, 23, 22, 25, 24, 27, 26, 29, 28, 31, 30]

                t_cos = b.dma("sync", cosb, ap["c_cos"])
                t_sin = b.dma("sync", sinb, ap["c_sin"])
                t_msk = b.dma("sync", maskb, ap["c_mask"])
                tv_ya0 = b.produce("vector", lambda e: e.memset(A[:, :], 0.0))
                b.wait("vector", t_cos); b.wait("vector", t_sin); b.wait("vector", t_msk)

                last_av = None
                last_op = None
                last_ex = None
                last_msk = None
                last_avev = None
                last_otev = None
                last_yev = None
                for g in range(8):
                    b.wait("sync", last_op)
                    t_wq = b.dma("sync", W3[:, 0:1024].rearrange("p (c f) -> p c f", f=128),
                                 ap["wqkv"][:, g * 128:(g + 1) * 128].rearrange("(c p) f -> p c f", p=128))
                    t_wk = b.dma("sync", W3[:, 1024:2048].rearrange("p (c f) -> p c f", f=128),
                                 ap["wqkv"][:, D + g * 128:D + (g + 1) * 128].rearrange("(c p) f -> p c f", p=128))
                    t_wv = b.dma("sync", W3[:, 2048:3072].rearrange("p (c f) -> p c f", f=128),
                                 ap["wqkv"][:, 2 * D + g * 128:2 * D + (g + 1) * 128].rearrange("(c p) f -> p c f", p=128))
                    t_wo = b.dma("sync", W3[0:64, 3072:4096], ap["wo"][g * 128:g * 128 + 64, :])
                    b.wait("tensor", t_wv); b.wait("tensor", last_av); b.wait("tensor", last_yev)
                    for dst_off, woff in ((0, 0), (2048, 1024)):
                        for tqc in range(4):
                            bank = pb[6 + (tqc % 2)]
                            for c in range(NC):
                                ttq = b.produce("tensor", lambda e, c=c, tqc=tqc, woff=woff, bank=bank: e.matmul(
                                    bank[:, 0:512], W3[:, woff + c * 128:woff + (c + 1) * 128],
                                    B[:, c * T + tqc * 512:c * T + (tqc + 1) * 512],
                                    start=(c == 0), stop=(c == NC - 1)))
                            b.wait("vector", ttq)
                            evq = b.produce("vector", lambda e, dst_off=dst_off, tqc=tqc, bank=bank: e.tensor_copy(
                                C3[:, dst_off + tqc * 512:dst_off + (tqc + 1) * 512], bank[:, 0:512]))
                            b.wait("tensor", evq)
                    for tqc in range(4):
                        bank = pb[6 + (tqc % 2)]
                        for c in range(NC):
                            ttv = b.produce("tensor", lambda e, c=c, tqc=tqc, bank=bank: e.matmul(
                                bank[:, 0:512], W3[:, 2048 + c * 128:2048 + (c + 1) * 128],
                                B[:, c * T + tqc * 512:c * T + (tqc + 1) * 512],
                                start=(c == 0), stop=(c == NC - 1)))
                        b.wait("vector", ttv)
                        evv = b.produce("vector", lambda e, tqc=tqc, bank=bank: e.tensor_copy(
                            tmp3[:, tqc * 512:(tqc + 1) * 512], bank[:, 0:512]))
                        b.wait("tensor", evv)
                    for i in range(NT):
                        bank = pb[6 + (i % 2)]
                        ttt = b.produce("tensor", lambda e, i=i, bank=bank: e.transpose(
                            bank[:, 0:128], tmp3[:, i * 128:(i + 1) * 128], ident[:, :]))
                        b.wait("vector", ttt)
                        evt = b.produce("vector", lambda e, i=i, bank=bank: e.tensor_copy(
                            VA[:, i * 130:i * 130 + 64], bank[:, 0:64]))
                        evt = b.produce("vector", lambda e, i=i, bank=bank: e.tensor_copy(
                            VA[:, i * 130 + 65:i * 130 + 129], bank[:, 64:128]))
                        b.wait("tensor", evt)
                    va_ones = VA.rearrange("p (i x) -> p i x", x=130)
                    b.produce("vector", lambda e: e.memset(va_ones[:, :, 64:65], 1.0))
                    ones_tok = b.produce("vector", lambda e: e.memset(va_ones[:, :, 129:130], 1.0))
                    b.wait("scalar", ones_tok)
                    tv_red = None
                    for qk, buf in ((0, QT), (1, KT)):
                        ts_sq2 = b.produce("scalar", lambda e, buf=buf: e.activation(tmp3, buf, AF.Square))
                        for j in range(4):
                            b.wait("tensor", ts_sq2)
                            if tv_red is not None:
                                b.wait("tensor", tv_red)
                            tt_cs = b.produce("tensor", lambda e, j=j: e.matmul(
                                pb[5][0:1, 0:512], ones[:, 0:1], tmp3[:, j * 512:(j + 1) * 512],
                                start=True, stop=True))
                            b.wait("vector", tt_cs)
                            tv_red = b.produce("vector", lambda e, qk=qk, j=j: e.tensor_reduce(
                                bsc[0:1, 4 * qk + j:4 * qk + j + 1] if False else smalls[0:1, 388 + 4 * qk + j:389 + 4 * qk + j],
                                pb[5][0:1, 0:512], AX.X, AL.max))
                        b.wait("scalar", tv_red); b.wait("scalar", tt_cs)
                    b.produce("vector", lambda e: e.tensor_reduce(bsc[0:1, 0:1], smalls[0:1, 388:392], AX.X, AL.max))
                    b.produce("vector", lambda e: e.tensor_reduce(bsc[0:1, 1:2], smalls[0:1, 392:396], AX.X, AL.max))
                    tvb = b.produce("vector", lambda e: e.tensor_tensor(bsc[0:1, 2:3], bsc[0:1, 0:1], bsc[0:1, 1:2], AL.mult))
                    b.wait("scalar", tvb)
                    tsb = b.produce("scalar", lambda e: e.activation(bsc[0:1, 2:3], bsc[0:1, 2:3], AF.Sqrt))
                    b.wait("vector", tsb)
                    tvb = b.produce("vector", lambda e: e.tensor_scalar(bsc[0:1, 2:3], bsc[0:1, 2:3], -0.125, None, AL.mult))
                    negb_ev = bcast_row(negb, bsc[0:1, 2:3], 1, pb[5], (tvb,))
                    b.wait("scalar", negb_ev)
                    b.wait("vector", tt_cs)
                    for buf in (QT, KT):
                        tvs = b.produce("vector", lambda e, buf=buf: e.stream_shuffle(tmp3, buf, SWAP))
                        b.produce("vector", lambda e: e.tensor_tensor(tmp3, tmp3, sinb, AL.mult))
                        b.produce("vector", lambda e, buf=buf: e.tensor_tensor(buf, buf, cosb, AL.mult))
                        tv_rp = b.produce("vector", lambda e, buf=buf: e.tensor_tensor(buf, buf, tmp3, AL.add))
                    b.wait("tensor", tv_rp)
                    b.wait("sync", tv_rp)
                    t_wo1 = b.dma("sync", tmp3[0:64, 1024:2048], ap["wo"][g * 128 + 64:(g + 1) * 128, :])
                    for tq in range(NT):
                        for h in range(2):
                            p0 = h * 64
                            b.wait("tensor", last_ex); b.wait("tensor", last_msk)
                            for tkt in range(tq + 1):
                                tts = b.produce("tensor", lambda e, tkt=tkt, tq=tq, p0=p0: e.matmul(
                                    pb[tkt // 4][:, (tkt % 4) * 128:((tkt % 4) + 1) * 128],
                                    KT[p0:p0 + 64, tkt * 128:(tkt + 1) * 128],
                                    QT[p0:p0 + 64, tq * 128:(tq + 1) * 128],
                                    start=True, stop=True, tile_position=(p0, 0)))
                            b.wait("vector", tts)
                            last_msk = b.produce("vector", lambda e, tq=tq: e.tensor_tensor(
                                pb[tq // 4][:, (tq % 4) * 128:((tq % 4) + 1) * 128],
                                pb[tq // 4][:, (tq % 4) * 128:((tq % 4) + 1) * 128], maskb, AL.add))
                            b.wait("scalar", last_msk); b.wait("scalar", tts)
                            nspan = tq + 1
                            for bk in range((nspan + 3) // 4):
                                w = min(4, nspan - bk * 4) * 128
                                last_ex = b.produce("scalar", lambda e, bk=bk, w=w: e.activation(
                                    PT[:, bk * 512:bk * 512 + w], pb[bk][:, 0:w], AF.Exp, bias=negb, scale=0.125))
                            b.wait("tensor", last_ex); b.wait("tensor", last_avev)
                            for tkt in range(tq + 1):
                                tta = b.produce("tensor", lambda e, tkt=tkt, h=h, tq=tq: e.matmul(
                                    pb[4][:, 0:65], PT[:, tkt * 128:(tkt + 1) * 128],
                                    VA[:, tkt * 130 + h * 65:tkt * 130 + (h + 1) * 65],
                                    start=(tkt == 0), stop=(tkt == tq)))
                            last_av = tta
                            b.wait("vector", tta)
                            b.produce("vector", lambda e: e.reciprocal(bsc[:, 2:3], pb[4][:, 64:65]))
                            last_avev = b.produce("vector", lambda e: e.tensor_scalar(o_sb, pb[4][:, 0:64], bsc[:, 2:3], None, AL.mult))
                            b.wait("tensor", last_avev); b.wait("tensor", last_otev)
                            tto = b.produce("tensor", lambda e: e.transpose(pb[5][0:64, 0:128], o_sb, ident[:, :]))
                            b.wait("vector", tto)
                            last_otev = b.produce("vector", lambda e: e.tensor_copy(oT_sb, pb[5][0:64, 0:128]))
                            b.wait("tensor", last_otev); b.wait("tensor", t_wo); b.wait("tensor", t_wo1)
                            if h == 0:
                                b.wait("tensor", last_yev)
                            wo_rhs = W3[0:64, 3072:4096] if h == 0 else tmp3[0:64, 1024:2048]
                            for dh in range(2):
                                ttp = b.produce("tensor", lambda e, h=h, dh=dh, wo_rhs=wo_rhs: e.matmul(
                                    pb[6 + dh][:, 0:512], oT_sb,
                                    wo_rhs[:, dh * 512:(dh + 1) * 512],
                                    start=(h == 0), stop=(h == 1)))
                            last_op = ttp
                            if h == 1:
                                b.wait("vector", ttp)
                                for dh in range(2):
                                    last_yev = b.produce("vector", lambda e, tq=tq, dh=dh: e.scalar_tensor_tensor(
                                        A[:, tq * D + dh * 512:tq * D + (dh + 1) * 512],
                                        pb[6 + dh][:, 0:512], 1.0,
                                        A[:, tq * D + dh * 512:tq * D + (dh + 1) * 512],
                                        AL.mult, AL.add))
                b.barrier()
                for i in range(NT):
                    transpose_B_tile_to(i, zsb)
                    b.produce("vector", lambda e, i=i: e.tensor_tensor(rsb, zsb, A[:, i * D:(i + 1) * D], AL.add))
                    tok = ln_stats_apply(rsb, A[:, i * D:(i + 1) * D])
                    b.wait("tensor", tok)
                    transpose_tile_to_B(i, A[:, i * D:(i + 1) * D])
                b.barrier()

        if stop_stage >= 4:
            with contextlib.ExitStack() as ph:
                C4 = ph.enter_context(nc.sbuf_tensor("C_ffn", [128, 8192], F32))
                W4 = ph.enter_context(nc.sbuf_tensor("W_ffn", [128, 4096], F32))
                h1q = C4[:, 0:4096]
                h2q = C4[:, 4096:8192]
                bcol = smalls[:, 0:8]
                t_fb = b.dma("sync", bcol, ap["fbias"].rearrange("(c p) -> p c", p=128), slow=True)

                last_wu = None
                last_hw = None
                last_tz = None
                last_fz = None
                for tq in range(4):
                    for m in range(3):
                        wname = ("fw1", "fa1", "fa2")[m]
                        for oh in range(2):
                            b.wait("sync", last_wu)
                            t_w = b.dma("sync", W4.rearrange("p (c f) -> p c f", f=512),
                                        ap[wname][:, oh * 512:(oh + 1) * 512].rearrange("(c p) f -> p c f", p=128))
                            b.wait("tensor", t_w)
                            for fs in range(4):
                                strip = oh * 4 + fs
                                bank = pb[fs % 2]
                                if m < 2:
                                    b.wait("tensor", last_hw)
                                else:
                                    b.wait("tensor", last_tz)
                                for c in range(NC):
                                    src_ap = (B[:, c * T + tq * 512:c * T + (tq + 1) * 512] if m == 0 else
                                              (h1q if m == 1 else h2q)[:, c * 512:(c + 1) * 512])
                                    ttf = b.produce("tensor", lambda e, c=c, fs=fs, bank=bank, src_ap=src_ap: e.matmul(
                                        bank[:, 0:512], W4[:, c * 512 + fs * 128:c * 512 + (fs + 1) * 128],
                                        src_ap, start=(c == 0), stop=(c == NC - 1)))
                                last_wu = ttf
                                if m < 2:
                                    dst = (h1q if m == 0 else h2q)[:, strip * 512:(strip + 1) * 512]
                                    b.wait("vector", ttf)
                                    last_hw = b.produce("vector", lambda e, dst=dst, bank=bank: e.tensor_copy(dst, bank[:, 0:512]))
                                    b.wait("tensor", last_hw)
                                else:
                                    b.wait("scalar", ttf); b.wait("scalar", t_fb)
                                    b.wait("scalar", last_fz)
                                    last_tz = b.produce("scalar", lambda e, bank=bank, strip=strip: e.activation(
                                        zsb[:, 0:512], bank[:, 0:512], AF.Tanh, bias=bcol[:, strip:strip + 1]))
                                    b.wait("tensor", last_tz)
                                    for ti in range(4):
                                        tb = pb[2 + (ti % 2)]
                                        ttz = b.produce("tensor", lambda e, ti=ti, tb=tb: e.transpose(
                                            tb[:, 0:128], zsb[:, ti * 128:(ti + 1) * 128], ident[:, :]))
                                        b.wait("vector", ttz)
                                        last_fz = b.produce("vector", lambda e, ti=ti, tq=tq, strip=strip, tb=tb: e.tensor_copy(
                                            A[:, (tq * 4 + ti) * D + strip * 128:(tq * 4 + ti) * D + (strip + 1) * 128],
                                            tb[:, 0:128]))
                                        b.wait("tensor", last_fz)
                b.barrier()
                for i in range(NT):
                    transpose_B_tile_to(i, zsb)
                    b.produce("vector", lambda e, i=i: e.tensor_tensor(rsb, zsb, A[:, i * D:(i + 1) * D], AL.add))
                    ln_stats_apply(rsb, A[:, i * D:(i + 1) * D])
                b.barrier()

        st = b.dma("sync", out.rearrange("(n p) m -> p n m", p=128), A.rearrange("p (n m) -> p n m", m=D))
        b.wait("sync", st)

        with nc.Block() as block:
            b.play(block)
    return nc


def kernel(L0_attn_u, L0_attn_A_log, L0_attn_B, L0_attn_C, L0_attn_D, L0_attn_delta,
           L0_attn_gamma, L0_attn_beta, L0_attn_freq_align, L0_attn_reward,
           L0_ffn_router_kernel_DE, L0_ffn_kernel_gating_EDF, L0_ffn_kernel_up_EDF,
           L0_ffn_kernel_down_EFD, L1_attn_w_qkv, L1_attn_w_out, L1_ffn_weight,
           L1_ffn_aux_weight1, L1_ffn_aux_weight2, L1_ffn_bias):
    nc = build()
    consts = host_constants()
    shared = dict(
        A_log=np.ascontiguousarray(L0_attn_A_log),
        Cm=np.ascontiguousarray(L0_attn_C), Dv=np.ascontiguousarray(L0_attn_D),
        delta=np.ascontiguousarray(L0_attn_delta), gamma=np.ascontiguousarray(L0_attn_gamma),
        beta=np.ascontiguousarray(L0_attn_beta), freq=np.ascontiguousarray(L0_attn_freq_align),
        reward=np.ascontiguousarray(L0_attn_reward), router=np.ascontiguousarray(L0_ffn_router_kernel_DE),
        wg=np.ascontiguousarray(L0_ffn_kernel_gating_EDF), wu=np.ascontiguousarray(L0_ffn_kernel_up_EDF),
        wd=np.ascontiguousarray(L0_ffn_kernel_down_EFD), wqkv=np.ascontiguousarray(L1_attn_w_qkv),
        wo=np.ascontiguousarray(L1_attn_w_out), fw1=np.ascontiguousarray(L1_ffn_weight),
        fa1=np.ascontiguousarray(L1_ffn_aux_weight1), fa2=np.ascontiguousarray(L1_ffn_aux_weight2),
        fbias=np.ascontiguousarray(L1_ffn_bias), **consts)
    in_maps = [dict(u=np.ascontiguousarray(L0_attn_u[i]), **shared) for i in range(8)]
    res = run_bass_kernel_spmd(nc, in_maps, core_ids=list(range(8)))
    return np.stack([res.results[i]["out"] for i in range(8)], axis=0)


# version 03 (final)
# speedup vs baseline: 1.0068x; 1.0068x over previous
import contextlib
import numpy as np
import concourse.bass as bass
import concourse.mybir as mybir
import bass_rust
from concourse import library_config
from concourse.bass_utils import run_bass_kernel_spmd

F32 = mybir.dt.float32
AF = mybir.ActivationFunctionType
AL = mybir.AluOpType
AX = mybir.AxisListType

T = 2048
D = 1024
NT = T // 128
NC = D // 128
NST = 64
SB = 32
EPS = 1e-5
BIG = 1.0e30

_STOP_STAGE = 4

ENGINES = ("tensor", "vector", "scalar", "gpsimd", "sync")


class Bld:
    def __init__(self, nc):
        self.nc = nc
        self.ops = {e: [] for e in ENGINES}
        self.cnt = {}
        self.hiw = {}
        self.sems = {}
        self.dma_sems = ("QS", "QG", "QC")
        self.dma_wait_since = {}

    def sem_names(self):
        return ["T", "V", "S", "G", "QS", "QG", "QC"]

    def produce(self, eng, fn, sem=None, n=1):
        sem = sem or {"tensor": "T", "vector": "V", "scalar": "S", "gpsimd": "G"}[eng]
        if sem in ("T", "V", "S", "G"):
            cur = self.cnt.get(sem, 0)
            if cur > self.hiw.get((eng, sem), 0):
                self.hiw[(eng, sem)] = cur
                self.ops[eng].append(("wait", sem, cur, 0))
        self.cnt[sem] = self.cnt.get(sem, 0) + n
        self.ops[eng].append(("op", fn, sem, n))
        return (sem, self.cnt[sem])

    def emit(self, eng, fn):
        self.ops[eng].append(("op", fn, None, 0))

    def dma(self, qeng, out, in_, slow=False):
        sem = {"sync": "QS", "gpsimd": "QG", "scalar": "QC"}[qeng]
        if self.dma_wait_since.get(sem):
            self.ops[qeng].append(("wait", sem, self.cnt.get(sem, 0), 0))
            self.dma_wait_since[sem] = False
        return self.produce(qeng, lambda e, out=out, in_=in_, slow=slow: e.dma_start(out=out, in_=in_, allow_slow_non_contiguous=slow), sem=sem, n=16)

    def wait(self, eng, tok):
        if tok is None:
            return
        sem, c = tok
        if sem in self.dma_sems:
            c = self.cnt.get(sem, 0)
            self.dma_wait_since[sem] = True
        if self.hiw.get((eng, sem), 0) >= c:
            return
        self.hiw[(eng, sem)] = c
        self.ops[eng].append(("wait", sem, c, 0))

    def barrier(self):
        engines = ("tensor", "vector", "scalar", "gpsimd")
        toks = {e: self.produce(e, lambda eng: eng.nop()) for e in engines}
        for e in engines:
            for o, t in toks.items():
                if o != e:
                    self.wait(e, t)
        for q in ("sync",):
            for o, t in toks.items():
                self.wait(q, t)

    def play(self, block):
        sems = self.sems
        def mk(eng_name):
            def run(eng):
                for kind, a, bb, cc in self.ops[eng_name]:
                    if kind == "op":
                        h = a(eng)
                        if bb is not None:
                            h.then_inc(sems[bb], cc)
                    else:
                        eng.wait_ge(sems[a], bb)
            return run
        for e in ENGINES:
            if self.ops[e]:
                getattr(block, e)(mk(e))


def host_constants():
    c = {}
    c["c_ident"] = np.eye(128, dtype=np.float32)
    c["c_ones"] = np.ones((128, 128), dtype=np.float32)
    c["c_kidx"] = np.tile(np.arange(SB + 1, dtype=np.float32), (NST, 1))
    krev = (SB - 1) - (np.arange(128) % SB)
    c["c_krev"] = np.tile(krev[:, None].astype(np.float32), (1, NST))
    m = np.where(np.arange(128)[None, :] >= np.arange(128)[:, None], 0.0, -BIG)
    c["c_mask"] = m.astype(np.float32)
    inv = 1.0 / (10000.0 ** (np.arange(0, 64, 2, dtype=np.float64) / 64.0))
    tpos = np.arange(T, dtype=np.float64)
    fr = np.outer(inv, tpos)
    cos = np.cos(fr); sin = np.sin(fr)
    c2 = np.zeros((128, T)); s2 = np.zeros((128, T))
    for r in range(128):
        dh = r % 64; i = dh // 2
        c2[r] = cos[i]
        s2[r] = -sin[i] if dh % 2 == 0 else sin[i]
    c["c_cos"] = c2.astype(np.float32)
    c["c_sin"] = s2.astype(np.float32)
    return c


def build(stop_stage=None):
    if stop_stage is None:
        stop_stage = _STOP_STAGE
    nc = bass.Bass("TRN2", target_bir_lowering=False)
    ap = {}
    ins_shapes = {
        "u": [T, D],
        "A_log": [NST], "Cm": [D, NST], "Dv": [D],
        "delta": [D], "gamma": [D], "beta": [D], "freq": [D], "reward": [D],
        "router": [D, 8], "wg": [8, D, 2048], "wu": [8, D, 2048], "wd": [8, 2048, D],
        "wqkv": [D, 3 * D], "wo": [D, D],
        "fw1": [D, D], "fa1": [D, D], "fa2": [D, D], "fbias": [D],
        "c_ident": [128, 128], "c_ones": [128, 128], "c_kidx": [NST, SB + 1], "c_krev": [128, NST],
        "c_mask": [128, 128], "c_cos": [128, T], "c_sin": [128, T],
    }
    for k, s in ins_shapes.items():
        ap[k] = nc.dram_tensor(k, s, F32, kind="ExternalInput").ap()
    out = nc.dram_tensor("out", [T, D], F32, kind="ExternalOutput").ap()

    b = Bld(nc)
    es = contextlib.ExitStack()
    with es:
        for _sn in b.sem_names():
            b.sems[_sn] = es.enter_context(nc.semaphore(_sn))

        def sb(name, p, f):
            return es.enter_context(nc.sbuf_tensor(name, [p, f], F32))

        A = sb("A", 128, NT * D)
        B = sb("B", 128, NC * T)
        ident = sb("ident", 128, 128)
        ones = sb("ones", 128, 128)
        smalls = sb("smalls", 128, 1704)
        stat = sb("stat", 128, 16)
        Z = sb("Z", 128, 2048)
        pb = [es.enter_context(nc.psum_tensor(f"pb{i}", [128, 512], F32)) for i in range(8)]
        zsb = Z[:, 0:1024]
        rsb = Z[:, 1024:2048]

        def bcast_row(dst_ap, src_row_ap, n, bank, wait_toks=()):
            for tk in wait_toks:
                b.wait("tensor", tk)
            b.wait("tensor", t_ones)
            o0 = 0
            ev = None
            while o0 < n:
                w = min(512, n - o0)
                tt = b.produce("tensor", lambda e, o0=o0, w=w: e.matmul(
                    bank[:, 0:w], ones[0:1, :], src_row_ap[:, o0:o0 + w],
                    start=True, stop=True))
                b.wait("vector", tt)
                ev = b.produce("vector", lambda e, o0=o0, w=w: e.tensor_copy(dst_ap[:, o0:o0 + w], bank[:, 0:w]))
                b.wait("tensor", ev)
                o0 += w
            return ev

        def ln_stats_apply(src_ap, out_ap, scale_bcast=None, bias_bcast=None):
            b.produce("vector", lambda e: e.bn_stats(stat[:, 0:6], src_ap[:, 0:512]))
            b.produce("vector", lambda e: e.bn_stats(stat[:, 6:12], src_ap[:, 512:1024]))
            b.produce("vector", lambda e: e.bn_aggr(stat[:, 12:14], stat[:, 0:12]))
            b.produce("vector", lambda e: e.tensor_scalar(stat[:, 14:15], stat[:, 13:14], EPS, None, AL.add))
            tv_rc = b.produce("vector", lambda e: e.reciprocal(stat[:, 15:16], stat[:, 14:15]))
            b.wait("scalar", tv_rc)
            ts_sq = b.produce("scalar", lambda e: e.activation(stat[:, 14:15], stat[:, 15:16], AF.Sqrt))
            b.wait("vector", ts_sq)
            tok = b.produce("vector", lambda e: e.tensor_scalar(out_ap, src_ap, stat[:, 12:13], stat[:, 14:15], AL.subtract, AL.mult))
            if scale_bcast is not None:
                tok = b.produce("vector", lambda e: e.tensor_tensor(out_ap, out_ap, scale_bcast, AL.mult))
            if bias_bcast is not None:
                tok = b.produce("vector", lambda e: e.tensor_tensor(out_ap, out_ap, bias_bcast, AL.add))
            return tok

        def transpose_tile_to_B(i, src_ap):
            for c in range(NC):
                bank = pb[c // 4]
                tt = b.produce("tensor", lambda e, c=c, bank=bank: e.transpose(
                    bank[:, (c % 4) * 128:((c % 4) + 1) * 128],
                    src_ap[:, c * 128:(c + 1) * 128], ident[:, :]))
            b.wait("vector", tt)
            for half in range(2):
                ev = b.produce("vector", lambda e, half=half, i=i: e.tensor_copy(
                    B.rearrange("p (c t) -> p c t", t=T)[:, 4 * half:4 * half + 4, i * 128:(i + 1) * 128],
                    pb[half][:, :].rearrange("p (c t) -> p c t", t=128)))
            b.wait("tensor", ev)
            return ev

        def transpose_B_tile_to(i, dst_ap):
            for c in range(NC):
                bank = pb[c // 4]
                tt = b.produce("tensor", lambda e, c=c, i=i, bank=bank: e.transpose(
                    bank[:, (c % 4) * 128:((c % 4) + 1) * 128],
                    B[:, c * T + i * 128:c * T + (i + 1) * 128], ident[:, :]))
            b.wait("vector", tt)
            for half in range(2):
                ev = b.produce("vector", lambda e, half=half: e.tensor_copy(
                    dst_ap[:, half * 512:(half + 1) * 512], pb[half][:, :]))
            b.wait("tensor", ev)
            return ev

        t_u = b.dma("sync", A.rearrange("p (n m) -> p n m", m=D), ap["u"].rearrange("(n p) m -> p n m", p=128))
        t_ident = b.dma("sync", ident[:, :], ap["c_ident"])
        t_ones = b.dma("sync", ones[:, :], ap["c_ones"])

        with contextlib.ExitStack() as ph:
            C = ph.enter_context(nc.sbuf_tensor("C_ssm", [128, 8192], F32))
            G = ph.enter_context(nc.sbuf_tensor("G_ssm", [128, 5120], F32))
            gamma_b = G[:, 0:1024]
            beta_b = G[:, 1024:2048]
            CT = G[0:NST, 2048:3072]
            Mt = G[0:NST, 3072:4096]
            Ssb = G[0:NST, 4096:5120]

            kidx = smalls[0:NST, 0:33]
            powT = smalls[0:NST, 33:66]
            krev = smalls[:, 66:130]
            rint = smalls[:, 130:194]
            lamb = smalls[:, 194:258]
            lamc = smalls[0:NST, 258:259]
            lamr = smalls[0:1, 259:323]
            kst = smalls[:, 323:579]
            k0d = smalls[:, 579:587]
            dcol = smalls[:, 587:595]
            fcol = smalls[:, 595:603]
            alogc = smalls[0:NST, 603:604]
            alogr = smalls[0:1, 604:668]
            del0 = smalls[0:1, 668:669]
            d0b = smalls[:, 669:670]
            negd0 = smalls[:, 670:671]
            dfre = smalls[:, 671:679]

            t_kidx = b.dma("sync", kidx, ap["c_kidx"])
            t_krev = b.dma("sync", krev, ap["c_krev"])
            t_alogc = b.dma("sync", alogc, ap["A_log"].rearrange("(n o) -> n o", o=1))
            t_alogr = b.dma("sync", alogr, ap["A_log"].rearrange("(o n) -> o n", o=1))
            t_del0 = b.dma("sync", del0, ap["delta"][0:1].rearrange("(o n) -> o n", o=1))
            t_dcol = b.dma("sync", dcol, ap["Dv"].rearrange("(c p) -> p c", p=128), slow=True)
            t_fcol = b.dma("sync", fcol, ap["freq"].rearrange("(c p) -> p c", p=128), slow=True)
            t_grow = b.dma("sync", G[0:1, 0:1024], ap["gamma"].rearrange("(o d) -> o d", o=1))
            t_brow = b.dma("sync", G[0:1, 1024:2048], ap["beta"].rearrange("(o d) -> o d", o=1))
            t_rew = b.dma("sync", smalls[0:1, 680:1704], ap["reward"].rearrange("(o d) -> o d", o=1))
            t_cm = b.dma("sync", B[:, 0:NC * NST].rearrange("p (c n) -> p c n", n=NST), ap["Cm"].rearrange("(c p) n -> p c n", p=128))

            tg_g = bcast_row(gamma_b, G[0:1, 0:1024], 1024, pb[3], (t_grow,))
            tg_b = bcast_row(beta_b, G[0:1, 1024:2048], 1024, pb[3], (t_brow,))

            b.wait("scalar", t_alogc)
            ts_e1 = b.produce("scalar", lambda e: e.activation(lamc, alogc, AF.Exp))
            b.wait("scalar", t_alogr)
            ts_e2 = b.produce("scalar", lambda e: e.activation(lamr, alogr, AF.Exp))
            tg_d0 = bcast_row(d0b, del0, 1, pb[3], (t_del0,))
            b.produce("vector", lambda e: e.tensor_scalar(negd0, d0b, -1.0, None, AL.mult))
            b.wait("vector", ts_e1)
            tv_lamc = b.produce("vector", lambda e: e.tensor_tensor(lamc, lamc, negd0[0:NST, :], AL.mult))
            b.wait("vector", ts_e2)
            tv_lamr = b.produce("vector", lambda e: e.tensor_scalar(lamr, lamr, negd0[0:1, :], None, AL.mult))
            tg_lamb = bcast_row(lamb, lamr, NST, pb[3], (tv_lamr,))

            b.wait("vector", t_kidx)
            tv_pk = b.produce("vector", lambda e: e.tensor_scalar(powT, kidx, lamc, None, AL.mult))
            b.wait("scalar", tv_pk)
            ts_pow = b.produce("scalar", lambda e: e.activation(powT, powT, AF.Exp))
            b.wait("vector", t_krev); b.wait("vector", tg_lamb)
            tv_rk = b.produce("vector", lambda e: e.tensor_tensor(rint, krev, lamb, AL.mult))
            b.wait("scalar", tv_rk)
            ts_rin = b.produce("scalar", lambda e: e.activation(rint, rint, AF.Exp))

            b.wait("tensor", t_cm); b.wait("tensor", t_ident)
            for c in range(NC):
                bank = pb[c // 4]
                tt = b.produce("tensor", lambda e, c=c, bank=bank: e.transpose(
                    bank[0:NST, (c % 4) * 128:((c % 4) + 1) * 128], B[:, c * NST:(c + 1) * NST], ident[:, :]))
            b.wait("vector", tt)
            tv_ct = b.produce("vector", lambda e: e.tensor_copy(CT[:, 0:512], pb[0][0:NST, :]))
            tv_ct = b.produce("vector", lambda e: e.tensor_copy(CT[:, 512:1024], pb[1][0:NST, :]))
            b.wait("tensor", tv_ct); b.wait("tensor", ts_pow)
            for c in range(NC):
                tt = b.produce("tensor", lambda e, c=c: e.matmul(
                    pb[2][:, c * 32:(c + 1) * 32], CT[0:NST, c * 128:(c + 1) * 128],
                    powT[:, 0:32], start=True, stop=True))
            b.wait("vector", tt); b.wait("vector", t_fcol)
            b.produce("vector", lambda e: e.tensor_copy(kst, pb[2][:, 0:256]))
            for c in range(NC):
                b.produce("vector", lambda e, c=c: e.tensor_scalar(
                    kst[:, 32 * c:32 * (c + 1)], kst[:, 32 * c:32 * (c + 1)], fcol[:, c:c + 1], None, AL.mult))
            kst3 = kst.rearrange("p (c k) -> p c k", k=32)
            b.wait("vector", t_dcol)
            b.produce("vector", lambda e: e.tensor_tensor(dfre, dcol, fcol, AL.mult))
            tv_k0 = b.produce("vector", lambda e: e.tensor_tensor(k0d, kst3[:, :, 0], dfre, AL.add))

            b.wait("tensor", t_u)
            for i in range(NT):
                transpose_tile_to_B(i, A[:, i * D:(i + 1) * D])

            tv_s0 = b.produce("vector", lambda e: e.memset(Ssb, 0.0))
            r32 = powT[:, SB:SB + 1]
            rexp = powT[:, 1:1 + SB]
            upd_stt = {0: None, 1: None}
            last_emit_ev = {}
            y0T3 = C.rearrange("p (c b k) -> p c b k", c=NC, k=SB)

            for half in range(2):
                hoff = half * 1024
                b.wait("vector", tv_k0)
                for c in range(NC):
                    b.produce("vector", lambda e, c=c, hoff=hoff: e.tensor_scalar(
                        C[:, c * 1024:(c + 1) * 1024], B[:, c * T + hoff:c * T + hoff + 1024],
                        k0d[:, c:c + 1], None, AL.mult))
                uT3 = B.rearrange("p (c t) -> p c t", t=T)
                for c in range(NC):
                    for dlt in range(1, SB):
                        b.produce("vector", lambda e, c=c, dlt=dlt, hoff=hoff: e.scalar_tensor_tensor(
                            y0T3[:, c, :, dlt:SB],
                            uT3[:, c, hoff:hoff + 1024].rearrange("p (bb k) -> p bb k", k=SB)[:, :, 0:SB - dlt],
                            kst[:, 32 * c + dlt:32 * c + dlt + 1],
                            y0T3[:, c, :, dlt:SB], AL.mult, AL.add))

                for it in range(8):
                    i = half * 8 + it
                    for q in range(4):
                        g = i * 4 + q
                        b.wait("vector", tv_s0)
                        tv_mt = b.produce("vector", lambda e: e.tensor_tensor(Mt, CT, Ssb, AL.mult))
                        tail_bank = pb[4 + (g % 2)]
                        tok = last_emit_ev.get(g % 2)
                        if tok is not None:
                            b.wait("tensor", tok)
                        b.wait("tensor", tv_mt)
                        for c in range(NC):
                            tt = b.produce("tensor", lambda e, c=c, tail_bank=tail_bank: e.matmul(
                                tail_bank[:, c * 32:(c + 1) * 32], Mt[0:NST, c * 128:(c + 1) * 128],
                                rexp, start=True, stop=True))
                        b.wait("vector", tt)
                        for c in range(NC):
                            evt = b.produce("vector", lambda e, c=c, it=it, q=q, tail_bank=tail_bank: e.scalar_tensor_tensor(
                                C[:, c * 1024 + it * 128 + q * SB: c * 1024 + it * 128 + (q + 1) * SB],
                                tail_bank[:, c * 32:(c + 1) * 32], fcol[:, c:c + 1],
                                C[:, c * 1024 + it * 128 + q * SB: c * 1024 + it * 128 + (q + 1) * SB],
                                AL.mult, AL.add))
                        last_emit_ev[g % 2] = evt
                        ub = (pb[6], pb[7]) if (g % 2) == 0 else (pb[2], pb[3])
                        if upd_stt[g % 2] is not None:
                            b.wait("tensor", upd_stt[g % 2])
                        b.wait("tensor", ts_rin)
                        for h in range(2):
                            tt2 = b.produce("tensor", lambda e, i=i, q=q, h=h, ub=ub: e.matmul(
                                ub[h][0:NST, 0:512], rint[q * SB:(q + 1) * SB, :],
                                A[q * SB:(q + 1) * SB, i * D + h * 512:i * D + (h + 1) * 512],
                                start=True, stop=True, tile_position=(q * SB, 0)))
                        b.wait("vector", tt2)
                        for h in range(2):
                            _stt = b.produce("vector", lambda e, h=h, ub=ub: e.scalar_tensor_tensor(
                                Ssb[:, h * 512:(h + 1) * 512], Ssb[:, h * 512:(h + 1) * 512], r32,
                                ub[h][0:NST, 0:512], AL.mult, AL.add))
                        upd_stt[g % 2] = _stt

                for it in range(8):
                    i = half * 8 + it
                    for c in range(NC):
                        bank = pb[c // 4]
                        tt = b.produce("tensor", lambda e, c=c, it=it, bank=bank: e.transpose(
                            bank[:, (c % 4) * 128:((c % 4) + 1) * 128],
                            C[:, c * 1024 + it * 128:c * 1024 + (it + 1) * 128], ident[:, :]))
                    b.wait("vector", tt)
                    for hh in range(2):
                        tv_z = b.produce("vector", lambda e, hh=hh: e.tensor_copy(zsb[:, hh * 512:(hh + 1) * 512], pb[hh][:, :]))
                    b.wait("tensor", tv_z)
                    if i == 0:
                        b.wait("vector", t_rew)
                        b.produce("vector", lambda e: e.tensor_tensor(zsb[0:1, :], zsb[0:1, :], smalls[0:1, 680:1704], AL.add))
                    b.wait("vector", tg_g); b.wait("vector", tg_b)
                    ln_stats_apply(zsb, zsb, gamma_b, beta_b)
                    b.produce("vector", lambda e, i=i: e.tensor_tensor(rsb, zsb, A[:, i * D:(i + 1) * D], AL.add))
                    tok = ln_stats_apply(rsb, A[:, i * D:(i + 1) * D])
                    b.wait("tensor", tok)
                    transpose_tile_to_B(i, A[:, i * D:(i + 1) * D])
            b.barrier()

        if stop_stage >= 2:
            with contextlib.ExitStack() as ph:
                C2 = ph.enter_context(nc.sbuf_tensor("C_moe", [128, 6144], F32))
                W = ph.enter_context(nc.sbuf_tensor("W_moe", [128, 6144], F32))
                accq = C2[:, 0:2048]
                hq = C2[:, 2048:6144]
                fw_all = smalls[:, 64:192]
                scr = smalls[:, 192:256]

                t_rt = b.dma("sync", smalls[:, 0:64].rearrange("p (c e) -> p c e", e=8), ap["router"].rearrange("(c p) e -> p c e", p=128))
                b.wait("tensor", t_rt)
                tvcp = None
                for i in range(NT):
                    if tvcp is not None:
                        b.wait("tensor", tvcp)
                    for c in range(NC):
                        tt = b.produce("tensor", lambda e, i=i, c=c: e.matmul(
                            pb[6][:, 0:8], B[:, c * T + i * 128:c * T + (i + 1) * 128],
                            smalls[:, c * 8:(c + 1) * 8], start=(c == 0), stop=(c == NC - 1)))
                    b.wait("vector", tt)
                    tvcp = b.produce("vector", lambda e: e.tensor_copy(scr[:, 0:8], pb[6][:, 0:8]))
                    tvv = b.produce("vector", lambda e: e.tensor_reduce(scr[:, 8:9], scr[:, 0:8], AX.X, AL.max, negate=True))
                    b.wait("scalar", tvv)
                    tss = b.produce("scalar", lambda e: e.activation(scr[:, 0:8], scr[:, 0:8], AF.Exp, bias=scr[:, 8:9]))
                    b.wait("vector", tss)
                    b.produce("vector", lambda e: e.tensor_reduce(scr[:, 9:10], scr[:, 0:8], AX.X, AL.add))
                    b.produce("vector", lambda e: e.reciprocal(scr[:, 10:11], scr[:, 9:10]))
                    b.produce("vector", lambda e: e.tensor_scalar(scr[:, 0:8], scr[:, 0:8], scr[:, 10:11], None, AL.mult))
                    b.produce("vector", lambda e: e.tensor_reduce(scr[:, 8:9], scr[:, 0:8], AX.X, AL.max))
                    b.produce("vector", lambda e: e.tensor_scalar(scr[:, 16:24], scr[:, 0:8], scr[:, 8:9], None, AL.is_equal))
                    b.produce("vector", lambda e: e.scalar_tensor_tensor(scr[:, 24:32], scr[:, 16:24], -BIG, scr[:, 0:8], AL.mult, AL.add))
                    b.produce("vector", lambda e: e.tensor_reduce(scr[:, 9:10], scr[:, 24:32], AX.X, AL.max))
                    b.produce("vector", lambda e: e.tensor_scalar(scr[:, 24:32], scr[:, 24:32], scr[:, 9:10], None, AL.is_equal))
                    tvv = b.produce("vector", lambda e: e.tensor_tensor(scr[:, 11:12], scr[:, 9:10], scr[:, 8:9], AL.subtract))
                    b.wait("scalar", tvv)
                    tss = b.produce("scalar", lambda e: e.activation(scr[:, 11:12], scr[:, 11:12], AF.Exp))
                    b.wait("vector", tss)
                    b.produce("vector", lambda e: e.tensor_scalar(scr[:, 12:13], scr[:, 11:12], 1.0, None, AL.add))
                    b.produce("vector", lambda e: e.reciprocal(scr[:, 13:14], scr[:, 12:13]))
                    b.produce("vector", lambda e: e.tensor_tensor(scr[:, 14:15], scr[:, 11:12], scr[:, 13:14], AL.mult))
                    b.produce("vector", lambda e, i=i: e.tensor_scalar(fw_all[:, i * 8:(i + 1) * 8], scr[:, 16:24], scr[:, 13:14], None, AL.mult))
                    tvf = b.produce("vector", lambda e, i=i: e.scalar_tensor_tensor(
                        fw_all[:, i * 8:(i + 1) * 8], scr[:, 24:32], scr[:, 14:15], fw_all[:, i * 8:(i + 1) * 8], AL.mult, AL.add))
                b.wait("tensor", tvf)

                wgu = [W[:, 0:2048], W[:, 2048:4096]]
                wdb = [W[:, 4096:5120], W[:, 5120:6144]]
                QQ = ["sync", "scalar"]
                wtok = {}
                wuse = {}
                seq = []
                for t8 in range(8):
                    for ex in range(8):
                        for f16 in range(16):
                            seq.append(("gu", ex, t8, f16))
                        for f16 in range(16):
                            seq.append(("wd", ex, t8, f16))
                def issue_load(idx):
                    if idx >= len(seq):
                        return
                    kind, ex, t8, f16 = seq[idx]
                    par = idx % 2
                    q = QQ[par]
                    if kind == "gu":
                        b.wait(q, wuse.get(("gu", par)))
                        b.dma(q, wgu[par][:, 0:1024].rearrange("p (c f) -> p c f", f=128),
                              ap["wg"][ex, :, f16 * 128:(f16 + 1) * 128].rearrange("(c p) f -> p c f", p=128))
                        t2 = b.dma(q, wgu[par][:, 1024:2048].rearrange("p (c f) -> p c f", f=128),
                                   ap["wu"][ex, :, f16 * 128:(f16 + 1) * 128].rearrange("(c p) f -> p c f", p=128))
                        wtok[("gu", par)] = t2
                    else:
                        b.wait(q, wuse.get(("wd", par)))
                        wtok[("wd", par)] = b.dma(q, wdb[par], ap["wd"][ex, f16 * 128:(f16 + 1) * 128, :])
                issue_load(0); issue_load(1)
                li = 2
                ts_si = {}
                tv_h = {}
                dn_ev = None
                for t8 in range(8):
                    tv_acc0 = b.produce("vector", lambda e: e.memset(accq, 0.0))
                    for ex in range(8):
                        gbase = (t8 * 8 + ex) * 32
                        for f16 in range(16):
                            par = (gbase + f16) % 2
                            pg, pu = (pb[0], pb[1]) if (f16 % 2) == 0 else (pb[2], pb[3])
                            b.wait("tensor", wtok[("gu", par)])
                            b.wait("tensor", tv_h.get(f16 % 2))
                            b.wait("tensor", ts_si.get(f16 % 2))
                            for c in range(NC):
                                ttg = b.produce("tensor", lambda e, c=c, par=par, t8=t8, pg=pg: e.matmul(
                                    pg[:, 0:256], wgu[par][:, c * 128:(c + 1) * 128],
                                    B[:, c * T + t8 * 256:c * T + (t8 + 1) * 256],
                                    start=(c == 0), stop=(c == NC - 1)))
                            for c in range(NC):
                                ttu = b.produce("tensor", lambda e, c=c, par=par, t8=t8, pu=pu: e.matmul(
                                    pu[:, 0:256], wgu[par][:, 1024 + c * 128:1024 + (c + 1) * 128],
                                    B[:, c * T + t8 * 256:c * T + (t8 + 1) * 256],
                                    start=(c == 0), stop=(c == NC - 1)))
                            wuse[("gu", par)] = ttu
                            issue_load(li); li += 1
                            b.wait("scalar", ttg)
                            ts_si[f16 % 2] = b.produce("scalar", lambda e, f16=f16, pg=pg: e.activation(
                                rsb[:, (f16 % 2) * 512:(f16 % 2) * 512 + 256], pg[:, 0:256], AF.Silu))
                            b.wait("vector", ttu); b.wait("vector", ts_si[f16 % 2])
                            tv_h[f16 % 2] = b.produce("vector", lambda e, f16=f16, pu=pu: e.tensor_tensor(
                                hq[:, f16 * 256:(f16 + 1) * 256], pu[:, 0:256],
                                rsb[:, (f16 % 2) * 512:(f16 % 2) * 512 + 256], AL.mult))
                        b.wait("tensor", tv_h[0]); b.wait("tensor", tv_h[1])
                        if dn_ev is not None:
                            b.wait("tensor", dn_ev)
                        for f16 in range(16):
                            par = (gbase + 16 + f16) % 2
                            b.wait("tensor", wtok[("wd", par)])
                            for ti in range(2):
                                for dh in range(2):
                                    ttd = b.produce("tensor", lambda e, f16=f16, ti=ti, dh=dh, par=par: e.matmul(
                                        pb[4 + ti * 2 + dh][:, 0:512],
                                        hq[:, f16 * 256 + ti * 128:f16 * 256 + (ti + 1) * 128],
                                        wdb[par][:, dh * 512:(dh + 1) * 512],
                                        start=(f16 == 0), stop=(f16 == 15)))
                            wuse[("wd", par)] = ttd
                            issue_load(li); li += 1
                        b.wait("vector", ttd)
                        for ti in range(2):
                            for dh in range(2):
                                dn_ev = b.produce("vector", lambda e, ti=ti, dh=dh, t8=t8, ex=ex: e.scalar_tensor_tensor(
                                    accq[:, ti * 1024 + dh * 512:ti * 1024 + (dh + 1) * 512],
                                    pb[4 + ti * 2 + dh][:, 0:512],
                                    fw_all[:, (t8 * 2 + ti) * 8 + ex:(t8 * 2 + ti) * 8 + ex + 1],
                                    accq[:, ti * 1024 + dh * 512:ti * 1024 + (dh + 1) * 512],
                                    AL.mult, AL.add))
                    b.wait("tensor", dn_ev)
                    for ti in range(2):
                        i = t8 * 2 + ti
                        ev = transpose_B_tile_to(i, zsb)
                        b.produce("vector", lambda e, ti=ti: e.tensor_tensor(rsb, zsb, accq[:, ti * 1024:(ti + 1) * 1024], AL.add))
                        tok = ln_stats_apply(rsb, A[:, i * D:(i + 1) * D])
                        b.wait("tensor", tok)
                        transpose_tile_to_B(i, A[:, i * D:(i + 1) * D])
                b.barrier()

        if stop_stage >= 3:
            with contextlib.ExitStack() as ph:
                C3 = ph.enter_context(nc.sbuf_tensor("C_att", [128, 8224], F32))
                W3 = ph.enter_context(nc.sbuf_tensor("W_att", [128, 4096], F32))
                G3 = ph.enter_context(nc.sbuf_tensor("G_att", [128, 4096], F32))
                QT = C3[:, 0:2048]
                KT = C3[:, 2048:4096]
                VA = C3[:, 4096:6176]
                tmp3 = C3[:, 6176:8224]
                cosb = G3[:, 0:2048]
                sinb = G3[:, 2048:4096]
                maskb = smalls[:, 256:384]
                negb = smalls[:, 384:385]
                bsc = smalls[:, 385:388]
                o_sb = smalls[:, 400:464]
                oT_sb = smalls[0:64, 464:592]
                PT = Z[:, 0:2048]
                SWAP = [1, 0, 3, 2, 5, 4, 7, 6, 9, 8, 11, 10, 13, 12, 15, 14,
                        17, 16, 19, 18, 21, 20, 23, 22, 25, 24, 27, 26, 29, 28, 31, 30]

                t_cos = b.dma("sync", cosb, ap["c_cos"])
                t_sin = b.dma("sync", sinb, ap["c_sin"])
                t_msk = b.dma("sync", maskb, ap["c_mask"])
                tv_ya0 = b.produce("vector", lambda e: e.memset(A[:, :], 0.0))
                b.wait("vector", t_cos); b.wait("vector", t_sin); b.wait("vector", t_msk)

                last_av = None
                last_op = None
                last_ex = None
                last_msk = None
                last_avev = None
                last_otev = None
                last_yev = None
                for g in range(8):
                    b.wait("sync", last_op)
                    t_wq = b.dma("sync", W3[:, 0:1024].rearrange("p (c f) -> p c f", f=128),
                                 ap["wqkv"][:, g * 128:(g + 1) * 128].rearrange("(c p) f -> p c f", p=128))
                    t_wk = b.dma("sync", W3[:, 1024:2048].rearrange("p (c f) -> p c f", f=128),
                                 ap["wqkv"][:, D + g * 128:D + (g + 1) * 128].rearrange("(c p) f -> p c f", p=128))
                    t_wv = b.dma("sync", W3[:, 2048:3072].rearrange("p (c f) -> p c f", f=128),
                                 ap["wqkv"][:, 2 * D + g * 128:2 * D + (g + 1) * 128].rearrange("(c p) f -> p c f", p=128))
                    t_wo = b.dma("sync", W3[0:64, 3072:4096], ap["wo"][g * 128:g * 128 + 64, :])
                    b.wait("tensor", t_wv); b.wait("tensor", last_av); b.wait("tensor", last_yev)
                    for dst_off, woff in ((0, 0), (2048, 1024)):
                        for tqc in range(4):
                            bank = pb[6 + (tqc % 2)]
                            for c in range(NC):
                                ttq = b.produce("tensor", lambda e, c=c, tqc=tqc, woff=woff, bank=bank: e.matmul(
                                    bank[:, 0:512], W3[:, woff + c * 128:woff + (c + 1) * 128],
                                    B[:, c * T + tqc * 512:c * T + (tqc + 1) * 512],
                                    start=(c == 0), stop=(c == NC - 1)))
                            b.wait("vector", ttq)
                            evq = b.produce("vector", lambda e, dst_off=dst_off, tqc=tqc, bank=bank: e.tensor_copy(
                                C3[:, dst_off + tqc * 512:dst_off + (tqc + 1) * 512], bank[:, 0:512]))
                            b.wait("tensor", evq)
                    for tqc in range(4):
                        bank = pb[6 + (tqc % 2)]
                        for c in range(NC):
                            ttv = b.produce("tensor", lambda e, c=c, tqc=tqc, bank=bank: e.matmul(
                                bank[:, 0:512], W3[:, 2048 + c * 128:2048 + (c + 1) * 128],
                                B[:, c * T + tqc * 512:c * T + (tqc + 1) * 512],
                                start=(c == 0), stop=(c == NC - 1)))
                        b.wait("vector", ttv)
                        evv = b.produce("vector", lambda e, tqc=tqc, bank=bank: e.tensor_copy(
                            tmp3[:, tqc * 512:(tqc + 1) * 512], bank[:, 0:512]))
                        b.wait("tensor", evv)
                    for i in range(NT):
                        bank = pb[6 + (i % 2)]
                        ttt = b.produce("tensor", lambda e, i=i, bank=bank: e.transpose(
                            bank[:, 0:128], tmp3[:, i * 128:(i + 1) * 128], ident[:, :]))
                        b.wait("vector", ttt)
                        evt = b.produce("vector", lambda e, i=i, bank=bank: e.tensor_copy(
                            VA[:, i * 130:i * 130 + 64], bank[:, 0:64]))
                        evt = b.produce("vector", lambda e, i=i, bank=bank: e.tensor_copy(
                            VA[:, i * 130 + 65:i * 130 + 129], bank[:, 64:128]))
                        b.wait("tensor", evt)
                    va_ones = VA.rearrange("p (i x) -> p i x", x=130)
                    b.produce("vector", lambda e: e.memset(va_ones[:, :, 64:65], 1.0))
                    ones_tok = b.produce("vector", lambda e: e.memset(va_ones[:, :, 129:130], 1.0))
                    b.wait("scalar", ones_tok)
                    tv_red = None
                    for qk, buf in ((0, QT), (1, KT)):
                        ts_sq2 = b.produce("scalar", lambda e, buf=buf: e.activation(tmp3, buf, AF.Square))
                        for j in range(4):
                            b.wait("tensor", ts_sq2)
                            if tv_red is not None:
                                b.wait("tensor", tv_red)
                            tt_cs = b.produce("tensor", lambda e, j=j: e.matmul(
                                pb[5][0:1, 0:512], ones[:, 0:1], tmp3[:, j * 512:(j + 1) * 512],
                                start=True, stop=True))
                            b.wait("vector", tt_cs)
                            tv_red = b.produce("vector", lambda e, qk=qk, j=j: e.tensor_reduce(
                                bsc[0:1, 4 * qk + j:4 * qk + j + 1] if False else smalls[0:1, 388 + 4 * qk + j:389 + 4 * qk + j],
                                pb[5][0:1, 0:512], AX.X, AL.max))
                        b.wait("scalar", tv_red); b.wait("scalar", tt_cs)
                    b.produce("vector", lambda e: e.tensor_reduce(bsc[0:1, 0:1], smalls[0:1, 388:392], AX.X, AL.max))
                    b.produce("vector", lambda e: e.tensor_reduce(bsc[0:1, 1:2], smalls[0:1, 392:396], AX.X, AL.max))
                    tvb = b.produce("vector", lambda e: e.tensor_tensor(bsc[0:1, 2:3], bsc[0:1, 0:1], bsc[0:1, 1:2], AL.mult))
                    b.wait("scalar", tvb)
                    tsb = b.produce("scalar", lambda e: e.activation(bsc[0:1, 2:3], bsc[0:1, 2:3], AF.Sqrt))
                    b.wait("vector", tsb)
                    tvb = b.produce("vector", lambda e: e.tensor_scalar(bsc[0:1, 2:3], bsc[0:1, 2:3], -0.125, None, AL.mult))
                    negb_ev = bcast_row(negb, bsc[0:1, 2:3], 1, pb[5], (tvb,))
                    b.wait("scalar", negb_ev)
                    b.wait("vector", tt_cs)
                    for buf in (QT, KT):
                        tvs = b.produce("vector", lambda e, buf=buf: e.stream_shuffle(tmp3, buf, SWAP))
                        b.produce("vector", lambda e: e.tensor_tensor(tmp3, tmp3, sinb, AL.mult))
                        b.produce("vector", lambda e, buf=buf: e.tensor_tensor(buf, buf, cosb, AL.mult))
                        tv_rp = b.produce("vector", lambda e, buf=buf: e.tensor_tensor(buf, buf, tmp3, AL.add))
                    b.wait("tensor", tv_rp)
                    b.wait("sync", tv_rp)
                    t_wo1 = b.dma("sync", tmp3[0:64, 1024:2048], ap["wo"][g * 128 + 64:(g + 1) * 128, :])
                    for tq in range(NT):
                        for h in range(2):
                            p0 = h * 64
                            b.wait("tensor", last_ex); b.wait("tensor", last_msk)
                            for tkt in range(tq + 1):
                                tts = b.produce("tensor", lambda e, tkt=tkt, tq=tq, p0=p0: e.matmul(
                                    pb[tkt // 4][:, (tkt % 4) * 128:((tkt % 4) + 1) * 128],
                                    KT[p0:p0 + 64, tkt * 128:(tkt + 1) * 128],
                                    QT[p0:p0 + 64, tq * 128:(tq + 1) * 128],
                                    start=True, stop=True, tile_position=(p0, 0)))
                            b.wait("vector", tts)
                            last_msk = b.produce("vector", lambda e, tq=tq: e.tensor_tensor(
                                pb[tq // 4][:, (tq % 4) * 128:((tq % 4) + 1) * 128],
                                pb[tq // 4][:, (tq % 4) * 128:((tq % 4) + 1) * 128], maskb, AL.add))
                            b.wait("scalar", last_msk); b.wait("scalar", tts)
                            nspan = tq + 1
                            for bk in range((nspan + 3) // 4):
                                w = min(4, nspan - bk * 4) * 128
                                last_ex = b.produce("scalar", lambda e, bk=bk, w=w: e.activation(
                                    PT[:, bk * 512:bk * 512 + w], pb[bk][:, 0:w], AF.Exp, bias=negb, scale=0.125))
                            b.wait("tensor", last_ex); b.wait("tensor", last_avev)
                            for tkt in range(tq + 1):
                                tta = b.produce("tensor", lambda e, tkt=tkt, h=h, tq=tq: e.matmul(
                                    pb[4][:, 0:65], PT[:, tkt * 128:(tkt + 1) * 128],
                                    VA[:, tkt * 130 + h * 65:tkt * 130 + (h + 1) * 65],
                                    start=(tkt == 0), stop=(tkt == tq)))
                            last_av = tta
                            b.wait("vector", tta)
                            b.produce("vector", lambda e: e.reciprocal(bsc[:, 2:3], pb[4][:, 64:65]))
                            last_avev = b.produce("vector", lambda e: e.tensor_scalar(o_sb, pb[4][:, 0:64], bsc[:, 2:3], None, AL.mult))
                            b.wait("tensor", last_avev); b.wait("tensor", last_otev)
                            tto = b.produce("tensor", lambda e: e.transpose(pb[5][0:64, 0:128], o_sb, ident[:, :]))
                            b.wait("vector", tto)
                            last_otev = b.produce("vector", lambda e: e.tensor_copy(oT_sb, pb[5][0:64, 0:128]))
                            b.wait("tensor", last_otev); b.wait("tensor", t_wo); b.wait("tensor", t_wo1)
                            if h == 0:
                                b.wait("tensor", last_yev)
                            wo_rhs = W3[0:64, 3072:4096] if h == 0 else tmp3[0:64, 1024:2048]
                            for dh in range(2):
                                ttp = b.produce("tensor", lambda e, h=h, dh=dh, wo_rhs=wo_rhs: e.matmul(
                                    pb[6 + dh][:, 0:512], oT_sb,
                                    wo_rhs[:, dh * 512:(dh + 1) * 512],
                                    start=(h == 0), stop=(h == 1)))
                            last_op = ttp
                            if h == 1:
                                b.wait("vector", ttp)
                                for dh in range(2):
                                    last_yev = b.produce("vector", lambda e, tq=tq, dh=dh: e.scalar_tensor_tensor(
                                        A[:, tq * D + dh * 512:tq * D + (dh + 1) * 512],
                                        pb[6 + dh][:, 0:512], 1.0,
                                        A[:, tq * D + dh * 512:tq * D + (dh + 1) * 512],
                                        AL.mult, AL.add))
                b.barrier()
                for i in range(NT):
                    transpose_B_tile_to(i, zsb)
                    b.produce("vector", lambda e, i=i: e.tensor_tensor(rsb, zsb, A[:, i * D:(i + 1) * D], AL.add))
                    tok = ln_stats_apply(rsb, A[:, i * D:(i + 1) * D])
                    b.wait("tensor", tok)
                    transpose_tile_to_B(i, A[:, i * D:(i + 1) * D])
                b.barrier()

        if stop_stage >= 4:
            with contextlib.ExitStack() as ph:
                C4 = ph.enter_context(nc.sbuf_tensor("C_ffn", [128, 8192], F32))
                W4 = ph.enter_context(nc.sbuf_tensor("W_ffn", [128, 4096], F32))
                h1q = C4[:, 0:4096]
                h2q = C4[:, 4096:8192]
                bcol = smalls[:, 0:8]
                t_fb = b.dma("sync", bcol, ap["fbias"].rearrange("(c p) -> p c", p=128), slow=True)

                last_wu = None
                last_hw = None
                last_tz = None
                last_fz = None
                for tq in range(4):
                    for m in range(3):
                        wname = ("fw1", "fa1", "fa2")[m]
                        for oh in range(2):
                            b.wait("sync", last_wu)
                            t_w = b.dma("sync", W4.rearrange("p (c f) -> p c f", f=512),
                                        ap[wname][:, oh * 512:(oh + 1) * 512].rearrange("(c p) f -> p c f", p=128))
                            b.wait("tensor", t_w)
                            for fs in range(4):
                                strip = oh * 4 + fs
                                bank = pb[fs % 2]
                                if m < 2:
                                    b.wait("tensor", last_hw)
                                else:
                                    b.wait("tensor", last_tz)
                                for c in range(NC):
                                    src_ap = (B[:, c * T + tq * 512:c * T + (tq + 1) * 512] if m == 0 else
                                              (h1q if m == 1 else h2q)[:, c * 512:(c + 1) * 512])
                                    ttf = b.produce("tensor", lambda e, c=c, fs=fs, bank=bank, src_ap=src_ap: e.matmul(
                                        bank[:, 0:512], W4[:, c * 512 + fs * 128:c * 512 + (fs + 1) * 128],
                                        src_ap, start=(c == 0), stop=(c == NC - 1)))
                                last_wu = ttf
                                if m < 2:
                                    dst = (h1q if m == 0 else h2q)[:, strip * 512:(strip + 1) * 512]
                                    b.wait("vector", ttf)
                                    last_hw = b.produce("vector", lambda e, dst=dst, bank=bank: e.tensor_copy(dst, bank[:, 0:512]))
                                    b.wait("tensor", last_hw)
                                else:
                                    b.wait("scalar", ttf); b.wait("scalar", t_fb)
                                    b.wait("scalar", last_fz)
                                    last_tz = b.produce("scalar", lambda e, bank=bank, strip=strip: e.activation(
                                        zsb[:, 0:512], bank[:, 0:512], AF.Tanh, bias=bcol[:, strip:strip + 1]))
                                    b.wait("tensor", last_tz)
                                    for ti in range(4):
                                        tb = pb[2 + (ti % 2)]
                                        ttz = b.produce("tensor", lambda e, ti=ti, tb=tb: e.transpose(
                                            tb[:, 0:128], zsb[:, ti * 128:(ti + 1) * 128], ident[:, :]))
                                        b.wait("vector", ttz)
                                        last_fz = b.produce("vector", lambda e, ti=ti, tq=tq, strip=strip, tb=tb: e.tensor_copy(
                                            A[:, (tq * 4 + ti) * D + strip * 128:(tq * 4 + ti) * D + (strip + 1) * 128],
                                            tb[:, 0:128]))
                                        b.wait("tensor", last_fz)
                b.barrier()
                for i in range(NT):
                    transpose_B_tile_to(i, zsb)
                    b.produce("vector", lambda e, i=i: e.tensor_tensor(rsb, zsb, A[:, i * D:(i + 1) * D], AL.add))
                    ln_stats_apply(rsb, A[:, i * D:(i + 1) * D])
                b.barrier()

        st = b.dma("sync", out.rearrange("(n p) m -> p n m", p=128), A.rearrange("p (n m) -> p n m", m=D))
        b.wait("sync", st)

        with nc.Block() as block:
            b.play(block)
    return nc


def kernel(L0_attn_u, L0_attn_A_log, L0_attn_B, L0_attn_C, L0_attn_D, L0_attn_delta,
           L0_attn_gamma, L0_attn_beta, L0_attn_freq_align, L0_attn_reward,
           L0_ffn_router_kernel_DE, L0_ffn_kernel_gating_EDF, L0_ffn_kernel_up_EDF,
           L0_ffn_kernel_down_EFD, L1_attn_w_qkv, L1_attn_w_out, L1_ffn_weight,
           L1_ffn_aux_weight1, L1_ffn_aux_weight2, L1_ffn_bias):
    nc = build()
    consts = host_constants()
    shared = dict(
        A_log=np.ascontiguousarray(L0_attn_A_log),
        Cm=np.ascontiguousarray(L0_attn_C), Dv=np.ascontiguousarray(L0_attn_D),
        delta=np.ascontiguousarray(L0_attn_delta), gamma=np.ascontiguousarray(L0_attn_gamma),
        beta=np.ascontiguousarray(L0_attn_beta), freq=np.ascontiguousarray(L0_attn_freq_align),
        reward=np.ascontiguousarray(L0_attn_reward), router=np.ascontiguousarray(L0_ffn_router_kernel_DE),
        wg=np.ascontiguousarray(L0_ffn_kernel_gating_EDF), wu=np.ascontiguousarray(L0_ffn_kernel_up_EDF),
        wd=np.ascontiguousarray(L0_ffn_kernel_down_EFD), wqkv=np.ascontiguousarray(L1_attn_w_qkv),
        wo=np.ascontiguousarray(L1_attn_w_out), fw1=np.ascontiguousarray(L1_ffn_weight),
        fa1=np.ascontiguousarray(L1_ffn_aux_weight1), fa2=np.ascontiguousarray(L1_ffn_aux_weight2),
        fbias=np.ascontiguousarray(L1_ffn_bias), **consts)
    in_maps = [dict(u=np.ascontiguousarray(L0_attn_u[i]), **shared) for i in range(8)]
    res = run_bass_kernel_spmd(nc, in_maps, core_ids=list(range(8)))
    return np.stack([res.results[i]["out"] for i in range(8)], axis=0)
